# version 11
# baseline (speedup 1.0000x reference)
import numpy as np
import ml_dtypes
from contextlib import ExitStack

import concourse.bass as bass
import concourse.mybir as mybir
from concourse.bass_utils import run_bass_kernel_spmd

F32 = mybir.dt.float32
BF16 = mybir.dt.bfloat16
I32 = mybir.dt.int32
AF = mybir.ActivationFunctionType
ALU = mybir.AluOpType
AX = mybir.AxisListType.X

D = 1024
E = 1536
E2 = 3072
H = 4096
S = 4096
TM = 512
NTM = S // TM
TF = 256
NTF = S // TF
MAGIC = 0x5F3759DF
NW = D * H

CMPSEM = {"sync": None, "tensor": "pe", "scalar": "act", "vector": "dve",
          "gpsimd": "pool"}


class Sched:
    ENGINES = ("sync", "tensor", "scalar", "vector", "gpsimd")

    def __init__(self):
        self.q = {e: [] for e in self.ENGINES}
        self.cnt = {}
        self.ev = {}
        self.hwm = {e: {} for e in self.ENGINES}
        self.slots = {}
        self.auto = 0

    def _resolve(self, eng, waits):
        out = []
        for w in waits:
            if w is None:
                continue
            if isinstance(w, tuple):
                sem, val, peng, kind = w[0], w[1], None, "dma"
            else:
                sem, val, peng, kind = self.ev[w]
            if kind == "cmp" and peng == eng:
                continue
            if self.hwm[eng].get(sem, 0) >= val:
                continue
            self.hwm[eng][sem] = val
            out.append((sem, val))
        return out

    def op(self, eng, emit, waits=(), mark=None, sem=None, amt=1, kind="cmp",
           reads=(), writes=()):
        waits = list(waits)
        for s in writes:
            sl = self.slots.get(s)
            if sl:
                if sl["w"] is not None:
                    waits.append(sl["w"])
                waits.extend(sl["r"])
        for s in reads:
            sl = self.slots.get(s)
            if sl and sl["w"] is not None:
                waits.append(sl["w"])
        if mark is None and (reads or writes):
            mark = f"_a{self.auto}"
            self.auto += 1
        if mark is not None:
            if sem is None:
                sem = CMPSEM[eng]
                assert sem is not None, "sync engine compute mark needs sem"
            self.cnt[sem] = self.cnt.get(sem, 0) + amt
            self.ev[mark] = (sem, self.cnt[sem], eng, kind)
        ws = self._resolve(eng, waits)
        marks = [(sem, amt)] if mark is not None else []
        self.q[eng].append((ws, emit, marks))
        for s in writes:
            self.slots[s] = {"w": mark, "r": []}
        for s in reads:
            sl = self.slots.setdefault(s, {"w": None, "r": []})
            sl["r"].append(mark)
        return mark

    def dma(self, eng, out, in_, waits=(), mark=None, sem=None,
            reads=(), writes=()):
        if mark is None:
            mark = f"_d{self.auto}"
            self.auto += 1
        return self.op(eng,
                       (lambda o_, i_: (lambda E: E.dma_start(out=o_, in_=i_)))(out, in_),
                       waits=waits, mark=mark, sem=sem, amt=16, kind="dma",
                       reads=reads, writes=writes)

    def fence(self, eng, sem, val):
        self.q[eng].append(([(sem, val)], None, []))
        self.hwm[eng][sem] = max(self.hwm[eng].get(sem, 0), val)


def build_program(debug_x2=False):
    nc = bass.Bass("TRN2", target_bir_lowering=False, debug=False)

    xT = nc.dram_tensor("xT", [D, S], F32, kind="ExternalInput").ap()
    whgT = nc.dram_tensor("whgT", [D, E2], F32, kind="ExternalInput").ap()
    woutT = nc.dram_tensor("woutT", [E, D], F32, kind="ExternalInput").ap()
    wfcT = nc.dram_tensor("wfcT", [D, H], F32, kind="ExternalInput").ap()
    wprojT = nc.dram_tensor("wprojT", [H, D], F32, kind="ExternalInput").ap()
    bhg2 = nc.dram_tensor("bhg2", [128, 24], F32, kind="ExternalInput").ap()
    bout2 = nc.dram_tensor("bout2", [128, 8], F32, kind="ExternalInput").ap()
    h02 = nc.dram_tensor("h02", [128, 12], F32, kind="ExternalInput").ap()
    lnw2 = nc.dram_tensor("lnw2", [128, 32], F32, kind="ExternalInput").ap()
    lnb2 = nc.dram_tensor("lnb2", [128, 32], F32, kind="ExternalInput").ap()
    onesf = nc.dram_tensor("onesf", [128, 128], F32, kind="ExternalInput").ap()
    ones8 = nc.dram_tensor("ones8", [128, 8], F32, kind="ExternalInput").ap()
    onesb = nc.dram_tensor("onesb", [128, 128], BF16, kind="ExternalInput").ap()
    yT = nc.dram_tensor("yT", [D, S], F32, kind="ExternalOutput").ap()
    x2s = nc.dram_tensor("x2s", [D, S], F32,
                         kind="ExternalOutput" if debug_x2 else "Internal").ap()
    dbgkind = "ExternalOutput" if debug_x2 else "Internal"
    wq1d = nc.dram_tensor("wq1d", [D, H], BF16, kind=dbgkind).ap()
    wq2d = nc.dram_tensor("wq2d", [H, D], BF16, kind=dbgkind).ap()
    if debug_x2:
        lndbg = nc.dram_tensor("lndbg", [D, S], BF16, kind="ExternalOutput").ap()
        hdbg = nc.dram_tensor("hdbg", [E, S], BF16, kind="ExternalOutput").ap()
        cdbg = nc.dram_tensor("cdbg", [128, 12 * 8], F32, kind="ExternalOutput").ap()
        s12dbg = nc.dram_tensor("s12dbg", [128, 16], F32, kind="ExternalOutput").ap()
        bwdbg = nc.dram_tensor("bwdbg", [128, 32], F32, kind="ExternalOutput").ap()
        bw2dbg = nc.dram_tensor("bw2dbg", [128, 64], BF16, kind="ExternalOutput").ap()
        ldbg = nc.dram_tensor("ldbg", [D, S], BF16, kind="ExternalOutput").ap()
        udbg = nc.dram_tensor("udbg", [H, S], BF16, kind="ExternalOutput").ap()

    sch = Sched()
    stack = ExitStack()

    OFF_W = 0
    OFF_MX = 36864
    OFF_S1 = 65536
    OFF_S2 = 81920
    OFF_ST = 90112
    ARENA = 101376
    A = stack.enter_context(nc.sbuf_tensor("arena", [128, ARENA], BF16))

    def bview(off, n):
        return A[:, off:off + n]

    def fvw(off, n):
        return A[:, off:off + 2 * n].bitcast(F32)

    def stv(off_f32, n):
        return fvw(OFF_ST + 2 * off_f32, n)

    whg_v = [bview(OFF_W + k * E2, E2) for k in range(8)]
    wout_v = [bview(OFF_W + 24576 + k * D, D) for k in range(12)]
    wq1_v = [bview(OFF_W + k * H, H) for k in range(8)]
    wq2_v = [bview(OFF_W + 32768 + k * D, D) for k in range(32)]

    def mxf(t, p):
        return fvw(OFF_MX + t * 2048 + p * 1024, TM)
    CT = lambda p: mxf(0, p)
    ZT = lambda p: mxf(1, p)
    SGT = lambda p: mxf(2, p)
    GT = lambda p: mxf(3, p)
    VT = lambda p: mxf(4, p)
    HOT = lambda p: mxf(5, p)
    hbf_v = [bview(OFF_MX + 12288 + j * TM, TM) for j in range(12)]
    ln1_v2 = [[bview(OFF_MX + 18432 + ip * 4096 + k * TM, TM) for k in range(8)]
              for ip in range(2)]
    aWQ = lambda p: fvw(OFF_MX + 26624 + p * 1024, 512)
    aWQ2 = lambda p: stv(3072 + p * 512, 512)
    SX = lambda p, dc: fvw(OFF_S1 + p * 8192 + dc * 1024, TM)
    WB = lambda p: fvw(OFF_S1 + p * 8192, 4096)
    FX = lambda p, dc: fvw(OFF_S1 + p * 8192 + dc * 512, TF)
    UB = lambda hk: (bview(OFF_S1 + 4096 + hk * TF, TF) if hk < 16 else
                     bview(OFF_S1 + 12288 + (hk - 16) * TF, TF))
    X2T = lambda dm: fvw(OFF_S2 + dm * 1024, TM)
    QB = lambda p: bview(OFF_S2 + p * 1024, 1024)
    OT = lambda dm: fvw(OFF_S2 + dm * 512, TF)
    LB = lambda dc: bview(OFF_S2 + 4096 + dc * TF, TF)
    U2 = lambda p: bview(OFF_S2 + 6144 + p * TF, TF)
    SQ = lambda p: stv(p * 512, TM)
    SQF = lambda p: stv(p * 256, TF)
    maskWQ = stv(1024, 1024)
    sgnWQ = stv(2048, 1024)
    ti = stv(1024, 512)
    ti_i = bview(OFF_ST + 2 * 1024, 1024).bitcast(I32)
    ty = stv(1536, 512)
    ta = stv(2048, 512)
    vt = stv(2560, 512)
    YE = lambda q: stv(3072 + q * 512, TM)
    EPt = lambda q: stv(3072 + q * 256, TF)
    m1 = stv(4096, 512)
    r1 = stv(4608, 512)
    mr1 = stv(5120, 512)
    m2 = stv(4096, 256); hm = stv(4352, 256)
    r2 = stv(4608, 256); hr = stv(4864, 256)
    mr2 = stv(5120, 256); nhmr = stv(5376, 256)

    def small(name, shape, dt=F32):
        return stack.enter_context(nc.sbuf_tensor(name, shape, dt))
    sm_bhg = small("sm_bhg", [128, 24])
    sm_negbg = small("sm_negbg", [128, 12])
    sm_bh05 = small("sm_bh05", [128, 12])
    sm_bout = small("sm_bout", [128, 8])
    sm_h0 = small("sm_h0", [128, 12])
    sm_tmp = small("sm_tmp", [128, 12])
    carry = small("carry", [128, 12])
    sm_lnw = small("sm_lnw", [128, 32])
    sm_lnb = small("sm_lnb", [128, 32])
    sm_rw = small("sm_rw", [128, 32])
    sm_bw = small("sm_bw", [128, 32])
    sm_bwb = small("sm_bwb", [128, 32], BF16)
    sm_bwb2 = small("sm_bwb2", [128, 64], BF16)
    ones_f = small("ones_f", [128, 128])
    ones_8 = small("ones_8", [128, 8])
    ones_b = small("ones_b", [128, 128], BF16)
    magic = small("magic", [128, 1], I32)
    gpart = small("gpart", [128, 64])
    gpartB = small("gpartB", [128, 64])
    g2 = small("g2", [128, 2])
    gev = small("gev", [128, 2])
    gs = small("gs", [128, 2])
    sm_gam = small("sm_gam", [128, 2])
    sm_thr = small("sm_thr", [128, 2])
    sm_nthr = small("sm_nthr", [128, 2])
    sm_nlnw = small("sm_nlnw", [128, 32])
    s1sb = small("s1sb", [128, 8])
    s2sb = small("s2sb", [128, 8])

    pb = [stack.enter_context(nc.psum_tensor(f"pb{i}", [128, 512], F32))
          for i in range(8)]

    SEMS = ["pe", "act", "dve", "pool", "ld0", "ld1", "wld0", "wld1",
            "sm", "fw", "st", "qst"] + [f"sl{k}" for k in range(8)]

    o = sch.op
    dma = sch.dma

    for name, dst, src in [("bhg", sm_bhg, bhg2), ("bout", sm_bout, bout2),
                           ("h0", sm_h0, h02), ("lnw", sm_lnw, lnw2),
                           ("lnb", sm_lnb, lnb2), ("onesf", ones_f, onesf),
                           ("ones8", ones_8, ones8), ("onesb", ones_b, onesb)]:
        dma("sync", dst[:], src, mark=f"sm_{name}", sem="sm")
    SM_ALL = ("sm", 8 * 16)
    o("vector", lambda E: E.memset(magic[:], MAGIC))
    o("scalar", lambda E: E.activation(sm_tmp[:], sm_h0[:], AF.Sigmoid),
      waits=[SM_ALL], mark="pro_sig")
    o("vector", lambda E: E.tensor_scalar(carry[:], sm_h0[:], 0.5, None, ALU.add),
      waits=[SM_ALL])
    o("vector", lambda E: E.tensor_tensor(carry[:], carry[:], sm_tmp[:], ALU.max),
      waits=["pro_sig"], mark="pro_carry")
    o("vector", lambda E: E.tensor_scalar(sm_negbg[:], sm_bhg[:, 12:24], -1.0, None, ALU.mult))
    o("vector", lambda E: E.tensor_scalar(sm_bh05[:], sm_bhg[:, 0:12], 0.5, None, ALU.add))
    o("vector", lambda E: E.tensor_scalar(sm_nlnw[:], sm_lnw[:], -1.0, None, ALU.mult))
    o("vector", lambda E: E.reciprocal(sm_rw[:], sm_lnw[:]), mark="pro_rw")
    o("gpsimd", lambda E: E.tensor_tensor(sm_bw[:], sm_lnb[:], sm_rw[:], ALU.mult),
      waits=["pro_rw"])
    o("gpsimd", lambda E: E.tensor_copy(sm_bwb[:], sm_bw[:]))
    o("gpsimd", lambda E: E.memset(sm_bwb2[:], 1.0))
    o("gpsimd", lambda E: E.tensor_copy(
        sm_bwb2[:].rearrange("p (k t) -> p k t", t=2)[:, :, 1:2],
        sm_bw[:].rearrange("p (k o) -> p k o", o=1)), mark="pro_done")

    SL = lambda k: fvw(OFF_S1 + k * 2048, 1024)
    A_d = fvw(OFF_S2 + 0, 1024)
    A_p = fvw(OFF_S2 + 2048, 1024)
    MN_p = fvw(OFF_S2 + 4096, 1024)
    qb_d = bview(OFF_S2 + 6144, 1024)
    qb_p = bview(OFF_S2 + 7168, 1024)
    gslot = [0]

    def nslot():
        k = gslot[0] % 8
        gslot[0] += 1
        return k

    def wsub_src(s):
        if s < 32:
            dc, qq = s // 4, s % 4
            return wfcT[dc * 128:(dc + 1) * 128, qq * 1024:(qq + 1) * 1024]
        hk = s - 32
        return wprojT[hk * 128:(hk + 1) * 128, :]

    def mw_unit(m):
        k = nslot()
        if m < 24:
            dc, q = m // 3, m % 3
            dma("sync", SL(k)[:], whgT[dc * 128:(dc + 1) * 128, q * 1024:(q + 1) * 1024],
                sem=f"sl{k}", writes=[f"sl{k}"])
            o("scalar", (lambda kk, d, qq: (lambda E: E.copy(
                whg_v[d][:, qq * 1024:(qq + 1) * 1024], SL(kk)[:])))(k, dc, q),
              reads=[f"sl{k}"], mark=f"whg{dc}" if q == 2 else None)
        else:
            kj = m - 24
            dma("sync", SL(k)[:], woutT[kj * 128:(kj + 1) * 128, :],
                sem=f"sl{k}", writes=[f"sl{k}"])
            o("scalar", (lambda kk, d: (lambda E: E.copy(wout_v[d][:], SL(kk)[:])))(k, kj),
              reads=[f"sl{k}"], mark=f"wout{kj}")

    def p1_unit(s):
        k = nslot()
        dma("sync", SL(k)[:], wsub_src(s), sem=f"sl{k}", writes=[f"sl{k}"])
        o("vector", (lambda ss, kk: (lambda E: E.reduce_sum(
            gpart[:, ss:ss + 1], SL(kk)[:], AX, apply_absolute_value=True)))(s, k),
          reads=[f"sl{k}"], mark=f"wqr_{s}")

    def p2_unit(s):
        k = nslot()
        e = s % 2
        eng = "vector" if e else "gpsimd"
        w = 0 if s < 32 else 1
        dma("sync", SL(k)[:], wsub_src(s), sem=f"sl{k}", writes=[f"sl{k}"])
        if e:
            o(eng, (lambda kk, ww: (lambda E: E.tensor_single_scalar(
                A_d[:], SL(kk)[:], sm_thr[:, ww:ww + 1], ALU.is_gt)))(k, w),
              waits=[f"thr_{w}"], reads=[f"sl{k}"], writes=["aqd"])
            o(eng, (lambda kk, ww: (lambda E: E.scalar_tensor_tensor(
                A_d[:], SL(kk)[:], sm_nthr[:, ww:ww + 1], A_d[:],
                ALU.is_lt, ALU.subtract)))(k, w),
              reads=[f"sl{k}"], writes=["aqd"])
            if s < 32:
                o(eng, lambda E: E.tensor_scalar(qb_d[:], A_d[:], -1.0, None, ALU.mult),
                  reads=["aqd"], writes=["qbd"], mark=f"wqq_{s}")
            else:
                hk = s - 32
                o(eng, (lambda kk2: (lambda E: E.tensor_scalar(
                    qb_d[:], A_d[:], sm_nlnw[:, kk2:kk2 + 1], None, ALU.mult)))(hk),
                  reads=["aqd"], writes=["qbd"], mark=f"wqq_{s}")
            qb, qslot = qb_d, "qbd"
        else:
            o(eng, (lambda kk, ww: (lambda E: E.tensor_single_scalar(
                A_p[:], SL(kk)[:], sm_thr[:, ww:ww + 1], ALU.is_gt)))(k, w),
              waits=[f"thr_{w}"], reads=[f"sl{k}"], writes=["aqp"])
            o(eng, (lambda kk, ww: (lambda E: E.tensor_single_scalar(
                MN_p[:], SL(kk)[:], sm_nthr[:, ww:ww + 1], ALU.is_lt)))(k, w),
              reads=[f"sl{k}"], writes=["mnp"])
            o(eng, lambda E: E.tensor_tensor(A_p[:], A_p[:], MN_p[:], ALU.subtract),
              reads=["mnp"], writes=["aqp"])
            if s < 32:
                o(eng, lambda E: E.tensor_copy(qb_p[:], A_p[:]),
                  reads=["aqp"], writes=["qbp"], mark=f"wqq_{s}")
            else:
                hk = s - 32
                o(eng, (lambda kk2: (lambda E: E.tensor_scalar(
                    qb_p[:], A_p[:], sm_lnw[:, kk2:kk2 + 1], None, ALU.mult)))(hk),
                  reads=["aqp"], writes=["qbp"], mark=f"wqq_{s}")
            qb, qslot = qb_p, "qbp"
        if s < 32:
            dc, qq4 = s // 4, s % 4
            dma("gpsimd", wq1d[dc * 128:(dc + 1) * 128, qq4 * 1024:(qq4 + 1) * 1024],
                qb[:], sem="qst", reads=[qslot])
        else:
            hk = s - 32
            dma("gpsimd", wq2d[hk * 128:(hk + 1) * 128, :], qb[:],
                sem="qst", reads=[qslot])
            for dm in range(8):
                o("tensor", (lambda qq, dd, kk: (lambda E: E.matmul(
                    pb[dd][:, 0:2], qq[:, dd * 128:(dd + 1) * 128],
                    sm_bwb2[:, 2 * kk:2 * kk + 2], start=(kk == 0), stop=(kk == 31),
                    skip_group_check=True)))(qb, dm, hk),
                  waits=(["pro_done"] if (hk == 0 and dm == 0) else []),
                  reads=[qslot],
                  writes=([f"bank{dm}"] if hk == 0 else []),
                  mark=f"s1mm_{hk}_{dm}")

    mwq = list(range(36))
    p1q = list(range(64))
    while p1q or mwq:
        for _ in range(2):
            if p1q:
                p1_unit(p1q.pop(0))
        if mwq:
            mw_unit(mwq.pop(0))
    for w in range(2):
        o("gpsimd", (lambda ww: (lambda E: E.tensor_copy(
            gpartB[:, 32 * ww:32 * ww + 32], gpart[:, 32 * ww:32 * ww + 32])))(w),
          waits=[f"wqr_{32 * w + 30}", f"wqr_{32 * w + 31}"], mark=f"g2c_{w}")
        o("vector", (lambda ww: (lambda E: E.reduce_sum(
            g2[:, ww:ww + 1], gpartB[:, 32 * ww:32 * ww + 32], AX)))(w),
          waits=[f"g2c_{w}"], mark=f"g2_{w}")
        o("tensor", (lambda ww: (lambda E: E.matmul(
            pb[0][0:8, ww:ww + 1], ones_8[:], g2[:, ww:ww + 1], start=True, stop=True)))(w),
          waits=[f"g2_{w}", SM_ALL], writes=["bank0"], mark=f"gmm1_{w}")
        o("scalar", (lambda ww: (lambda E: E.copy(gev[0:8, ww:ww + 1], pb[0][0:8, ww:ww + 1])))(w),
          waits=[f"gmm1_{w}"], reads=["bank0"], mark=f"gev_{w}")
        o("tensor", (lambda ww: (lambda E: E.matmul(
            pb[0][0:1, ww:ww + 1], gev[0:8, ww:ww + 1], ones_f[0:8, 0:1], start=True, stop=True)))(w),
          waits=[f"gev_{w}"], writes=["bank0"], mark=f"gmm2_{w}")
        o("scalar", (lambda ww: (lambda E: E.copy(gs[0:1, ww:ww + 1], pb[0][0:1, ww:ww + 1])))(w),
          waits=[f"gmm2_{w}"], reads=["bank0"], mark=f"gs_{w}")
        o("tensor", (lambda ww: (lambda E: E.matmul(
            pb[0][:, ww:ww + 1], ones_f[0:1, 0:128], gs[0:1, ww:ww + 1], start=True, stop=True)))(w),
          waits=[f"gs_{w}"], writes=["bank0"], mark=f"gmm3_{w}")
        o("scalar", (lambda ww: (lambda E: E.activation(
            sm_gam[:, ww:ww + 1], pb[0][:, ww:ww + 1], AF.Copy,
            bias=1e-6, scale=1.0 / NW)))(w),
          waits=[f"gmm3_{w}"], reads=["bank0"], mark=f"gam_{w}")
        o("vector", (lambda ww: (lambda E: E.tensor_scalar(
            sm_thr[:, ww:ww + 1], sm_gam[:, ww:ww + 1], 0.5, None, ALU.mult)))(w),
          waits=[f"gam_{w}"])
        o("vector", (lambda ww: (lambda E: E.tensor_scalar(
            sm_nthr[:, ww:ww + 1], sm_gam[:, ww:ww + 1], -0.5, None, ALU.mult)))(w),
          mark=f"thr_{w}")
    for s in range(64):
        p2_unit(s)
    for dm in range(8):
        o("scalar", (lambda dd: (lambda E: E.copy(
            s1sb[:, dd:dd + 1], pb[dd][:, 0:1])))(dm),
          waits=[f"s1mm_31_{dm}"], reads=[f"bank{dm}"],
          mark=f"s1e_{dm}")
        o("scalar", (lambda dd: (lambda E: E.copy(
            s2sb[:, dd:dd + 1], pb[dd][:, 1:2])))(dm),
          reads=[f"bank{dm}"], mark=f"s2e_{dm}")
    S12_DONE = "s2e_7"
    if debug_x2:
        dma("gpsimd", s12dbg[:, 0:8], s1sb[:], waits=[S12_DONE], sem="st")
        dma("gpsimd", s12dbg[:, 8:16], s2sb[:], waits=[S12_DONE], sem="st")
        dma("gpsimd", bwdbg, sm_bw[:], waits=["pro_done"], sem="st")
        dma("gpsimd", bw2dbg, sm_bwb2[:], waits=["pro_done"], sem="st")

    QST_ALL = ("qst", 16 * 64)
    n_x2_stores = [0]
    n_dbg_stores = [4 if debug_x2 else 0]

    def m_load(i):
        p = i % 2
        for dc in range(8):
            dma("sync", SX(p, dc)[:], xT[dc * 128:(dc + 1) * 128, i * TM:(i + 1) * TM],
                waits=([QST_ALL] if i < 2 else []),
                sem=f"ld{p}", writes=[f"sx{p}_{dc}"], mark=f"xld_{i}_{dc}")

    def m_stats(i):
        p = i % 2
        for dc in range(8):
            o("scalar", (lambda pp, d: (lambda E: E.activation(
                SQ(d % 2)[:], SX(pp, d)[:], AF.Square)))(p, dc),
              reads=[f"sx{p}_{dc}"], writes=[f"msq{dc % 2}"], mark=f"m_sq_{i}_{dc}")
            o("tensor", (lambda pp, d: (lambda E: E.matmul(
                pb[6][:], ones_f[:], SX(pp, d)[:], start=(d == 0), stop=(d == 7),
                skip_group_check=True)))(p, dc),
              reads=[f"sx{p}_{dc}"], writes=(["bank6"] if dc == 0 else []),
              mark=f"m_st1_{i}" if dc == 7 else None)
            o("tensor", (lambda d: (lambda E: E.matmul(
                pb[7][:], ones_f[:], SQ(d % 2)[:], start=(d == 0), stop=(d == 7),
                skip_group_check=True)))(dc),
              reads=[f"msq{dc % 2}"], writes=(["bank7"] if dc == 0 else []),
              mark=f"m_st2_{i}" if dc == 7 else None)
        o("scalar", lambda E: E.activation(m1[:], pb[6][:], AF.Copy, scale=1.0 / D),
          waits=[f"m_st1_{i}"], reads=["bank6"], writes=["m1s"], mark=f"m_mu_{i}")
        o("scalar", lambda E: E.activation(mr1[:], m1[:], AF.Square),
          reads=["m1s"], writes=["mr1s"], mark=f"m_msq_{i}")

    def m_stats_back(i):
        p = i % 2
        o("vector", lambda E: E.tensor_scalar(vt[:], pb[7][:], 1.0 / D, 1e-5, ALU.mult, ALU.add),
          waits=[f"m_st2_{i}"], reads=["bank7"], mark=f"m_vt_{i}")
        o("vector", lambda E: E.tensor_sub(vt[:], vt[:], mr1[:]),
          reads=["mr1s"])
        o("vector", lambda E: E.tensor_single_scalar(ti_i[:, 0:512], vt[:].bitcast(I32), 1, ALU.arith_shift_right))
        o("vector", lambda E: E.scalar_tensor_tensor(
            ti_i[:, 0:512], ti_i[:, 0:512], -1, magic[:, 0:1].broadcast_to([128, 512]),
            ALU.mult, ALU.add))
        yv = ti_i[:, 0:512].bitcast(F32)
        for it in range(2):
            dst = r1[:] if it == 1 else yv
            o("vector", lambda E: E.tensor_mul(ta[:], vt[:], yv))
            o("vector", lambda E: E.tensor_mul(ta[:], ta[:], yv))
            o("vector", lambda E: E.tensor_scalar(ta[:], ta[:], -0.5, 1.5, ALU.mult, ALU.add))
            o("vector", (lambda dd: (lambda E: E.tensor_mul(dd, yv, ta[:])))(dst),
              writes=(["r1s"] if it == 1 else []),
              mark=f"m_r1_{i}" if it == 1 else None)
        o("vector", lambda E: E.tensor_mul(mr1[:], m1[:], r1[:]),
          reads=["m1s", "r1s"], writes=["mr1s"], mark=f"m_mr1_{i}")
        for dc in range(8):
            tmp = ti if dc % 2 == 0 else ty
            o("vector", (lambda pp, d, t: (lambda E: E.tensor_mul(t[:], SX(pp, d)[:], r1[:])))(p, dc, tmp),
              reads=[f"sx{p}_{dc}", "r1s"])
            o("vector", (lambda pp, d, t: (lambda E: E.tensor_sub(ln1_v2[pp][d][:], t[:], mr1[:])))(p, dc, tmp),
              reads=["mr1s"], writes=[f"ln1_{p}_{dc}"], mark=f"m_ln1_{i}_{dc}")
        if debug_x2:
            for dc in range(8):
                dma("gpsimd", lndbg[dc * 128:(dc + 1) * 128, i * TM:(i + 1) * TM],
                    ln1_v2[p][dc][:], sem="st", reads=[f"ln1_{p}_{dc}"])
                n_dbg_stores[0] += 1

    def m_mixer(i, hook5=None, hook7=None):
        for j in range(12):
            if j == 6 and hook5 is not None:
                hook5()
            if j == 8 and hook7 is not None:
                hook7()
            jp = j % 2
            ip = i % 2
            for dc in range(8):
                o("tensor", (lambda jj, d, b, q: (lambda E: E.matmul(
                    pb[b][:], whg_v[d][:, jj * 128:(jj + 1) * 128], ln1_v2[q][d][:],
                    start=(d == 0), stop=(d == 7), skip_group_check=True)))(j, dc, jp, ip),
                  waits=[f"whg{dc}"] if (i == 0 and j == 0) else [],
                  reads=[f"ln1_{ip}_{dc}"], writes=([f"bank{jp}"] if dc == 0 else []),
                  mark=f"m_Ah_{i}_{j}" if dc == 7 else None)
            for dc in range(8):
                o("tensor", (lambda jj, d, b, q: (lambda E: E.matmul(
                    pb[2 + b][:], whg_v[d][:, (12 + jj) * 128:(13 + jj) * 128], ln1_v2[q][d][:],
                    start=(d == 0), stop=(d == 7), skip_group_check=True)))(j, dc, jp, ip),
                  reads=[f"ln1_{ip}_{dc}"], writes=([f"bank{2 + jp}"] if dc == 0 else []),
                  mark=f"m_Ag_{i}_{j}" if dc == 7 else None)
            o("scalar", (lambda jj, b: (lambda E: E.activation(
                SGT(b)[:], pb[b][:], AF.Sigmoid, bias=sm_bhg[:, jj:jj + 1])))(j, jp),
              waits=[f"m_Ah_{i}_{j}", "pro_done"],
              writes=[f"sgt{jp}"], mark=f"m_sg_{i}_{j}")
            o("vector", (lambda jj, b: (lambda E: E.scalar_tensor_tensor(
                GT(b)[:], pb[b][:], sm_bh05[:, jj:jj + 1], SGT(b)[:],
                ALU.add, ALU.max)))(j, jp),
              waits=[f"m_Ah_{i}_{j}"],
              reads=[f"sgt{jp}", f"bank{jp}"], writes=[f"gt{jp}"], mark=f"m_g_{i}_{j}")
            o("scalar", (lambda jj, b: (lambda E: E.activation(
                CT(b)[:], pb[2 + b][:], AF.Sigmoid, scale=-1.0,
                bias=sm_negbg[:, jj:jj + 1])))(j, jp),
              waits=[f"m_Ag_{i}_{j}"],
              writes=[f"ct{jp}"], mark=f"m_c_{i}_{j}")
            o("scalar", (lambda jj, b: (lambda E: E.activation(
                ZT(b)[:], pb[2 + b][:], AF.Sigmoid, bias=sm_bhg[:, 12 + jj:13 + jj])))(j, jp),
              reads=[f"bank{2 + jp}"], writes=[f"zt{jp}"], mark=f"m_z_{i}_{j}")
            o("gpsimd", (lambda b: (lambda E: E.tensor_mul(VT(b)[:], ZT(b)[:], GT(b)[:])))(jp),
              reads=[f"zt{jp}", f"gt{jp}"], writes=[f"vtm{jp}"], mark=f"m_v_{i}_{j}")
            o("vector", (lambda jj, b: (lambda E: E.tensor_tensor_scan(
                HOT(b)[:], CT(b)[:], VT(b)[:], carry[:, jj:jj + 1],
                ALU.mult, ALU.add)))(j, jp),
              waits=["pro_carry"] if i == 0 else [],
              reads=[f"ct{jp}", f"vtm{jp}", f"carry{j}"], writes=[f"hot{jp}"],
              mark=f"m_scan_{i}_{j}")
            o("gpsimd", (lambda jj, b: (lambda E: E.tensor_copy(
                carry[:, jj:jj + 1], HOT(b)[:, TM - 1:TM])))(j, jp),
              waits=[f"m_scan_{i}_{j}"],
              reads=[f"hot{jp}"], writes=[f"carry{j}"], mark=f"m_carry_{i}_{j}")
            o("gpsimd", (lambda jj, b: (lambda E: E.tensor_copy(hbf_v[jj][:], HOT(b)[:])))(j, jp),
              reads=[f"hot{jp}"], writes=[f"hbf{j}"], mark=f"m_hbf_{i}_{j}")
            if debug_x2:
                dma("gpsimd", hdbg[j * 128:(j + 1) * 128, i * TM:(i + 1) * TM],
                    hbf_v[j][:], sem="st", reads=[f"hbf{j}"])
                n_dbg_stores[0] += 1
        if debug_x2:
            dma("gpsimd", cdbg[:, i * 12:(i + 1) * 12], carry[:],
                waits=[f"m_scan_{i}_{11}"], sem="st")
            n_dbg_stores[0] += 1

    def m_out(i):
        p = i % 2
        for dm in range(8):
            bp = dm % 2
            for kj in range(12):
                o("tensor", (lambda d, k, b: (lambda E: E.matmul(
                    pb[4 + b][:], wout_v[k][:, d * 128:(d + 1) * 128], hbf_v[k][:],
                    start=(k == 0), stop=(k == 11), skip_group_check=True)))(dm, kj, bp),
                  waits=[f"wout{kj}"] if (i == 0 and dm == 0) else [],
                  reads=[f"hbf{kj}"], writes=([f"bank{4 + bp}"] if kj == 0 else []),
                  mark=f"m_B_{i}_{dm}" if kj == 11 else None)
            wl = [f"m_B_{i}_{dm}"]
            if dm == 0 and i > 0:
                wl.append(("st", 16 * (n_x2_stores[0] + n_dbg_stores[0])))
            if dm == 0 and i == 0:
                wl += [S12_DONE, QST_ALL]
            o("vector", (lambda pp, d, b: (lambda E: E.scalar_tensor_tensor(
                X2T(d)[:], pb[4 + b][:], sm_bout[:, d:d + 1], SX(pp, d)[:],
                ALU.add, ALU.add)))(p, dm, bp),
              waits=wl, reads=[f"bank{4 + bp}", f"sx{p}_{dm}"],
              mark=f"m_x2_{i}_{dm}")
            dma("gpsimd", x2s[dm * 128:(dm + 1) * 128, i * TM:(i + 1) * TM],
                X2T(dm)[:], waits=[f"m_x2_{i}_{dm}"], sem="st")
            n_x2_stores[0] += 1

    m_load(0)
    m_stats(0)
    m_stats_back(0)
    for i in range(NTM):
        if i + 1 < NTM:
            m_load(i + 1)
        m_mixer(i)
        if i + 1 < NTM:
            m_stats(i + 1)
        m_out(i)
        if i + 1 < NTM:
            m_stats_back(i + 1)
    X2_ALL = ("st", 16 * (n_x2_stores[0] + n_dbg_stores[0]))
    M_DONE = f"m_x2_{NTM - 1}_7"

    LAST = NTM - 1
    for dc in range(8):
        wl = [f"m_Ah_{LAST}_11", f"m_Ag_{LAST}_11", QST_ALL]
        if dc >= 6:
            wl.append(f"m_B_{LAST}_7")
        dma("sync", wq1_v[dc][:], wq1d[dc * 128:(dc + 1) * 128, :],
            waits=wl, sem="fw")
    for hk in range(32):
        dma("sync", wq2_v[hk][:], wq2d[hk * 128:(hk + 1) * 128, :],
            waits=[f"m_B_{LAST}_7", f"m_hbf_{LAST}_11", QST_ALL], sem="fw")
    FW1 = ("fw", 16 * 8)
    FW2 = ("fw", 16 * 40)

    n_out_stores = [0]

    def bankof(dm):
        return 4 + dm if dm < 4 else dm - 4

    def f_load(f):
        p = f % 2
        if f == 0:
            fwl = [("st", 16 * 8), f"m_x2_{NTM - 2}_7"]
        elif f == 1:
            fwl = [("st", 16 * 8), f"m_x2_{NTM - 1}_7"]
        else:
            fwl = []
        for dc in range(8):
            dma("sync", FX(p, dc)[:], x2s[dc * 128:(dc + 1) * 128, f * TF:(f + 1) * TF],
                waits=fwl,
                sem=f"ld{p}", writes=[f"fx{p}_{dc}"], mark=f"fxld_{f}_{dc}")

    def newton_f(src, dst, dstslot, mk):
        o("vector", (lambda s: (lambda E: E.tensor_single_scalar(
            ti_i[:, 0:TF], s.bitcast(I32), 1, ALU.arith_shift_right)))(src))
        o("vector", lambda E: E.scalar_tensor_tensor(
            ti_i[:, 0:TF], ti_i[:, 0:TF], -1, magic[:, 0:1].broadcast_to([128, TF]),
            ALU.mult, ALU.add))
        yv2 = ti_i[:, 0:TF].bitcast(F32)
        for it in range(2):
            dd = dst if it == 1 else yv2
            o("vector", (lambda s: (lambda E: E.tensor_mul(ta[:, 0:TF], s, yv2)))(src))
            o("vector", lambda E: E.tensor_mul(ta[:, 0:TF], ta[:, 0:TF], yv2))
            o("vector", lambda E: E.tensor_scalar(ta[:, 0:TF], ta[:, 0:TF], -0.5, 1.5, ALU.mult, ALU.add))
            o("vector", (lambda d2: (lambda E: E.tensor_mul(d2, yv2, ta[:, 0:TF])))(dd),
              writes=([dstslot] if it == 1 else []),
              mark=mk if it == 1 else None)

    def f_stats(f):
        p = f % 2
        for dc in range(8):
            o("scalar", (lambda pp, d: (lambda E: E.activation(
                SQF(d % 2)[:], FX(pp, d)[:], AF.Square)))(p, dc),
              reads=[f"fx{p}_{dc}"], writes=[f"fsq{dc % 2}"], mark=f"f_sq_{f}_{dc}")
            o("tensor", (lambda pp, d: (lambda E: E.matmul(
                pb[6][:, 0:TF], ones_f[:], FX(pp, d)[:], start=(d == 0), stop=(d == 7),
                skip_group_check=True)))(p, dc),
              reads=[f"fx{p}_{dc}"], writes=(["bank6"] if dc == 0 else []),
              mark=f"f_st1_{f}" if dc == 7 else None)
            o("tensor", (lambda d: (lambda E: E.matmul(
                pb[7][:, 0:TF], ones_f[:], SQF(d % 2)[:], start=(d == 0), stop=(d == 7),
                skip_group_check=True)))(dc),
              reads=[f"fsq{dc % 2}"], writes=(["bank7"] if dc == 0 else []),
              mark=f"f_st2_{f}" if dc == 7 else None)
        o("scalar", lambda E: E.activation(m2[:], pb[6][:, 0:TF], AF.Copy, scale=1.0 / D),
          waits=[f"f_st1_{f}"], reads=["bank6"], writes=["m2s"], mark=f"f_mu_{f}")
        o("scalar", lambda E: E.activation(mr2[:], m2[:], AF.Square),
          reads=["m2s"], writes=["mr2s"], mark=f"f_msq_{f}")
        o("vector", lambda E: E.tensor_scalar(vt[:, 0:TF], pb[7][:, 0:TF], 1.0 / D, 1e-5, ALU.mult, ALU.add),
          waits=[f"f_st2_{f}"], reads=["bank7"], mark=f"f_vt_{f}")
        o("vector", lambda E: E.tensor_sub(vt[:, 0:TF], vt[:, 0:TF], mr2[:]),
          reads=["mr2s"])
        newton_f(vt[:, 0:TF], r2[:], "r2s", f"f_r2_{f}")
        o("vector", lambda E: E.tensor_mul(mr2[:], m2[:], r2[:]),
          reads=["m2s", "r2s"], writes=["mr2s"], mark=f"f_mr2_{f}")
        for dc in range(8):
            tmp = ti if dc % 2 == 0 else ty
            o("vector", (lambda pp, d, t: (lambda E: E.tensor_mul(t[:, 0:TF], FX(pp, d)[:], r2[:])))(p, dc, tmp),
              reads=[f"fx{p}_{dc}", "r2s"])
            o("vector", (lambda d, t: (lambda E: E.tensor_sub(LB(d)[:], t[:, 0:TF], mr2[:])))(dc, tmp),
              reads=["mr2s"], writes=[f"lb{dc}"], mark=f"f_ln2_{f}_{dc}")
        if debug_x2:
            for dc in range(8):
                dma("gpsimd", ldbg[dc * 128:(dc + 1) * 128, f * TF:(f + 1) * TF],
                    LB(dc)[:], sem="st", reads=[f"lb{dc}"])
                n_dbg_stores[0] += 1

    def f_mid(f):
        def su_pair(k):
            kp = k % 2
            o("tensor", (lambda kk: (lambda E: E.matmul(
                pb[2][:, 0:TF], ones_b[:], UB(kk)[:], start=(kk == 0), stop=(kk == 31),
                skip_group_check=True)))(k),
              reads=[f"ub{k}"], writes=(["bank2"] if k == 0 else []),
              mark=f"f_su_{f}" if k == 31 else None)
            o("tensor", (lambda kk, b: (lambda E: E.matmul(
                pb[3][:, 0:TF], ones_b[:], U2(b)[:], start=(kk == 0), stop=(kk == 31),
                skip_group_check=True)))(k, kp),
              reads=[f"u2{kp}"], writes=(["bank3"] if k == 0 else []),
              mark=f"f_su2_{f}" if k == 31 else None)

        for hk in range(32):
            hp = hk % 2
            for dc in range(8):
                o("tensor", (lambda k, d, b: (lambda E: E.matmul(
                    pb[b][:, 0:TF], wq1_v[d][:, k * 128:(k + 1) * 128], LB(d)[:],
                    start=(d == 0), stop=(d == 7), skip_group_check=True)))(hk, dc, hp),
                  waits=[FW1] if (f == 0 and hk == 0 and dc == 0) else [],
                  reads=[f"lb{dc}"], writes=([f"bank{hp}"] if dc == 0 else []),
                  mark=f"f_F1_{f}_{hk}" if dc == 7 else None)
            o("scalar", (lambda k, b: (lambda E: E.activation(UB(k)[:], pb[b][:, 0:TF], AF.Gelu)))(hk, hp),
              waits=[f"f_F1_{f}_{hk}"],
              reads=[f"bank{hp}"], writes=[f"ub{hk}"], mark=f"f_gel_{f}_{hk}")
            o("scalar", (lambda k, b: (lambda E: E.activation(U2(b)[:], UB(k)[:], AF.Square)))(hk, hp),
              reads=[f"ub{hk}"], writes=[f"u2{hp}"], mark=f"f_usq_{f}_{hk}")
            if hk >= 1:
                su_pair(hk - 1)
        su_pair(31)
        if debug_x2:
            for hk2 in range(32):
                dma("gpsimd", udbg[hk2 * 128:(hk2 + 1) * 128, f * TF:(f + 1) * TF],
                    UB(hk2)[:], sem="st", reads=[f"ub{hk2}"])
                n_dbg_stores[0] += 1
        o("scalar", lambda E: E.activation(hm[:], pb[2][:, 0:TF], AF.Copy, scale=1.0 / H),
          waits=[f"f_su_{f}"], reads=["bank2"], writes=["hms"], mark=f"f_hmu_{f}")
        o("scalar", lambda E: E.activation(nhmr[:], hm[:], AF.Square),
          reads=["hms"], writes=["nhmrs"], mark=f"f_hmsq_{f}")
        o("vector", lambda E: E.tensor_scalar(vt[:, 0:TF], pb[3][:, 0:TF], 1.0 / H, 1e-5, ALU.mult, ALU.add),
          waits=[f"f_su2_{f}"], reads=["bank3"], mark=f"f_hvt_{f}")
        o("vector", lambda E: E.tensor_sub(vt[:, 0:TF], vt[:, 0:TF], nhmr[:]),
          reads=["nhmrs"])
        newton_f(vt[:, 0:TF], hr[:], "hrs", f"f_hr_{f}")
        o("vector", lambda E: E.tensor_mul(nhmr[:], hm[:], hr[:]),
          reads=["hms", "hrs"], writes=["nhmrs"])
        o("vector", lambda E: E.tensor_scalar(nhmr[:], nhmr[:], -1.0, None, ALU.mult),
          writes=["nhmrs"], mark=f"f_negmr_{f}")

    def f_f2(f, dms):
        for dm in dms:
            bk = bankof(dm)
            for hk in range(32):
                o("tensor", (lambda d, k, b: (lambda E: E.matmul(
                    pb[b][:, 0:TF], wq2_v[k][:, d * 128:(d + 1) * 128], UB(k)[:],
                    start=(k == 0), stop=(k == 31), skip_group_check=True)))(dm, hk, bk),
                  waits=[FW2] if (f == 0 and dm == 0 and hk == 0) else [],
                  reads=[f"ub{hk}"], writes=([f"bank{bk}"] if hk == 0 else []),
                  mark=f"f_F2_{f}_{dm}" if hk == 31 else None)

    def f_epi(f, dms):
        p = f % 2
        for dm in dms:
            bk = bankof(dm)
            ep = EPt(dm % 2)
            wl = [f"f_F2_{f}_{dm}", f"f_hr_{f}"]
            if dm == 0 and f > 0:
                wl.append(("st", 16 * (n_x2_stores[0] + n_dbg_stores[0] + n_out_stores[0])))
            o("vector", (lambda b, e: (lambda E: E.tensor_mul(e[:], pb[b][:, 0:TF], hr[:])))(bk, ep),
              waits=wl, reads=["hrs", f"bank{bk}"], writes=[f"ep{dm % 2}"])
            o("vector", (lambda d, e: (lambda E: E.scalar_tensor_tensor(
                e[:], nhmr[:], s1sb[:, d:d + 1], e[:], ALU.mult, ALU.add)))(dm, ep),
              waits=[S12_DONE, f"f_negmr_{f}"], reads=["nhmrs"])
            o("vector", (lambda pp, d, e: (lambda E: E.scalar_tensor_tensor(
                OT(d)[:], e[:], s2sb[:, d:d + 1], FX(pp, d)[:], ALU.add, ALU.add)))(p, dm, ep),
              reads=[f"fx{p}_{dm}", f"ep{dm % 2}"],
              mark=f"f_ep_{f}_{dm}")
            dma("gpsimd", yT[dm * 128:(dm + 1) * 128, f * TF:(f + 1) * TF],
                OT(dm)[:], waits=[f"f_ep_{f}_{dm}"], sem="st")
            n_out_stores[0] += 1

    f_load(0)
    f_stats(0)
    for f in range(NTF):
        if f + 1 < NTF:
            f_load(f + 1)
        f_mid(f)
        f_f2(f, [0, 1, 2, 3])
        f_epi(f, [0, 1, 2, 3])
        if f + 1 < NTF:
            f_stats(f + 1)
        f_f2(f, [4, 5, 6, 7])
        f_epi(f, [4, 5, 6, 7])
    sch.fence("sync", "st", 16 * (n_x2_stores[0] + n_dbg_stores[0] + n_out_stores[0]))

    semh = {}
    for sname in SEMS:
        semh[sname] = stack.enter_context(nc.semaphore(sname))
    block = stack.enter_context(nc.Block())

    def emitter(name):
        def run(E):
            for ws, emit, marks in sch.q[name]:
                for sem, val in ws:
                    E.wait_ge(semh[sem], val)
                if emit is not None:
                    ins = emit(E)
                    for sem, amt in marks:
                        ins.then_inc(semh[sem], amt)
        return run

    block.sync(emitter("sync"))
    block.tensor(emitter("tensor"))
    block.scalar(emitter("scalar"))
    block.vector(emitter("vector"))
    block.gpsimd(emitter("gpsimd"))
    stack.close()
    return nc


def kernel(**inputs):
    x = np.asarray(inputs["L0_attn_x"], np.float32)
    whgT = np.ascontiguousarray(np.asarray(inputs["L0_attn_W_hg"], np.float32).T)
    woutT = np.ascontiguousarray(np.asarray(inputs["L0_attn_W_out"], np.float32).T)
    wfcT = np.ascontiguousarray(np.asarray(inputs["L0_ffn_W_fc"], np.float32).T)
    wprojT = np.ascontiguousarray(np.asarray(inputs["L0_ffn_W_proj"], np.float32).T)
    bhg2 = np.ascontiguousarray(np.asarray(inputs["L0_attn_b_hg"], np.float32).reshape(24, 128).T)
    bout2 = np.ascontiguousarray(np.asarray(inputs["L0_attn_b_out"], np.float32).reshape(8, 128).T)
    h02 = np.ascontiguousarray(np.asarray(inputs["L0_attn_h0"], np.float32).reshape(12, 128).T)
    lnw2 = np.ascontiguousarray(np.asarray(inputs["L0_ffn_ln_w"], np.float32).reshape(32, 128).T)
    lnb2 = np.ascontiguousarray(np.asarray(inputs["L0_ffn_ln_b"], np.float32).reshape(32, 128).T)
    onesf = np.ones((128, 128), np.float32)
    ones8v = np.zeros((128, 8), np.float32)
    for m in range(8):
        ones8v[m * 16:(m + 1) * 16, m] = 1.0
    onesb = np.ones((128, 128), ml_dtypes.bfloat16)

    nc = build_program()
    shared = dict(whgT=whgT, woutT=woutT, wfcT=wfcT, wprojT=wprojT,
                  bhg2=bhg2, bout2=bout2, h02=h02, lnw2=lnw2, lnb2=lnb2,
                  onesf=onesf, ones8=ones8v, onesb=onesb)
    in_maps = []
    for c in range(8):
        m = dict(shared)
        m["xT"] = np.ascontiguousarray(x[c].T)
        in_maps.append(m)
    res = run_bass_kernel_spmd(nc, in_maps, core_ids=list(range(8)))
    out = np.stack([np.asarray(res.results[c]["yT"], np.float32).T
                    for c in range(8)], axis=0)
    return np.ascontiguousarray(out)


# version 12
# speedup vs baseline: 1.5156x; 1.5156x over previous
import numpy as np
import ml_dtypes
from contextlib import ExitStack

import concourse.bass as bass
import concourse.mybir as mybir
from concourse.bass_utils import run_bass_kernel_spmd

F32 = mybir.dt.float32
BF16 = mybir.dt.bfloat16
I32 = mybir.dt.int32
AF = mybir.ActivationFunctionType
ALU = mybir.AluOpType
AX = mybir.AxisListType.X

D = 1024
E = 1536
E2 = 3072
H = 4096
S = 4096
TM = 512
NTM = S // TM
TF = 256
NTF = S // TF
MAGIC = 0x5F3759DF
NW = D * H

CMPSEM = {"sync": None, "tensor": "pe", "scalar": "act", "vector": "dve",
          "gpsimd": "pool"}


class Sched:
    ENGINES = ("sync", "tensor", "scalar", "vector", "gpsimd")

    def __init__(self):
        self.q = {e: [] for e in self.ENGINES}
        self.cnt = {}
        self.ev = {}
        self.hwm = {e: {} for e in self.ENGINES}
        self.slots = {}
        self.auto = 0

    def _resolve(self, eng, waits):
        out = []
        for w in waits:
            if w is None:
                continue
            if isinstance(w, tuple):
                sem, val, peng, kind = w[0], w[1], None, "dma"
            else:
                sem, val, peng, kind = self.ev[w]
            if kind == "cmp" and peng == eng:
                continue
            if self.hwm[eng].get(sem, 0) >= val:
                continue
            self.hwm[eng][sem] = val
            out.append((sem, val))
        return out

    def op(self, eng, emit, waits=(), mark=None, sem=None, amt=1, kind="cmp",
           reads=(), writes=()):
        waits = list(waits)
        for s in writes:
            sl = self.slots.get(s)
            if sl:
                if sl["w"] is not None:
                    waits.append(sl["w"])
                waits.extend(sl["r"])
        for s in reads:
            sl = self.slots.get(s)
            if sl and sl["w"] is not None:
                waits.append(sl["w"])
        if mark is None and (reads or writes):
            mark = f"_a{self.auto}"
            self.auto += 1
        if mark is not None:
            if sem is None:
                sem = CMPSEM[eng]
                assert sem is not None, "sync engine compute mark needs sem"
            self.cnt[sem] = self.cnt.get(sem, 0) + amt
            self.ev[mark] = (sem, self.cnt[sem], eng, kind)
        ws = self._resolve(eng, waits)
        marks = [(sem, amt)] if mark is not None else []
        self.q[eng].append((ws, emit, marks))
        for s in writes:
            self.slots[s] = {"w": mark, "r": []}
        for s in reads:
            sl = self.slots.setdefault(s, {"w": None, "r": []})
            sl["r"].append(mark)
        return mark

    def dma(self, eng, out, in_, waits=(), mark=None, sem=None,
            reads=(), writes=()):
        if mark is None:
            mark = f"_d{self.auto}"
            self.auto += 1
        return self.op(eng,
                       (lambda o_, i_: (lambda E: E.dma_start(out=o_, in_=i_)))(out, in_),
                       waits=waits, mark=mark, sem=sem, amt=16, kind="dma",
                       reads=reads, writes=writes)

    def fence(self, eng, sem, val):
        self.q[eng].append(([(sem, val)], None, []))
        self.hwm[eng][sem] = max(self.hwm[eng].get(sem, 0), val)


def build_program(debug_x2=False):
    nc = bass.Bass("TRN2", target_bir_lowering=False, debug=False)

    xT = nc.dram_tensor("xT", [D, S], F32, kind="ExternalInput").ap()
    whgT = nc.dram_tensor("whgT", [D, E2], F32, kind="ExternalInput").ap()
    woutT = nc.dram_tensor("woutT", [E, D], F32, kind="ExternalInput").ap()
    wfcT = nc.dram_tensor("wfcT", [D, H], F32, kind="ExternalInput").ap()
    wprojT = nc.dram_tensor("wprojT", [H, D], F32, kind="ExternalInput").ap()
    bhg2 = nc.dram_tensor("bhg2", [128, 24], F32, kind="ExternalInput").ap()
    bout2 = nc.dram_tensor("bout2", [128, 8], F32, kind="ExternalInput").ap()
    h02 = nc.dram_tensor("h02", [128, 12], F32, kind="ExternalInput").ap()
    lnw2 = nc.dram_tensor("lnw2", [128, 32], F32, kind="ExternalInput").ap()
    lnb2 = nc.dram_tensor("lnb2", [128, 32], F32, kind="ExternalInput").ap()
    onesf = nc.dram_tensor("onesf", [128, 128], F32, kind="ExternalInput").ap()
    ones8 = nc.dram_tensor("ones8", [128, 8], F32, kind="ExternalInput").ap()
    onesb = nc.dram_tensor("onesb", [128, 128], BF16, kind="ExternalInput").ap()
    yT = nc.dram_tensor("yT", [D, S], F32, kind="ExternalOutput").ap()
    x2s = nc.dram_tensor("x2s", [D, S], F32,
                         kind="ExternalOutput" if debug_x2 else "Internal").ap()
    dbgkind = "ExternalOutput" if debug_x2 else "Internal"
    wq1d = nc.dram_tensor("wq1d", [D, H], BF16, kind=dbgkind).ap()
    wq2d = nc.dram_tensor("wq2d", [H, D], BF16, kind=dbgkind).ap()
    if debug_x2:
        lndbg = nc.dram_tensor("lndbg", [D, S], BF16, kind="ExternalOutput").ap()
        hdbg = nc.dram_tensor("hdbg", [E, S], BF16, kind="ExternalOutput").ap()
        cdbg = nc.dram_tensor("cdbg", [128, 12 * 8], F32, kind="ExternalOutput").ap()
        s12dbg = nc.dram_tensor("s12dbg", [128, 16], F32, kind="ExternalOutput").ap()
        bwdbg = nc.dram_tensor("bwdbg", [128, 32], F32, kind="ExternalOutput").ap()
        bw2dbg = nc.dram_tensor("bw2dbg", [128, 64], BF16, kind="ExternalOutput").ap()
        ldbg = nc.dram_tensor("ldbg", [D, S], BF16, kind="ExternalOutput").ap()
        udbg = nc.dram_tensor("udbg", [H, S], BF16, kind="ExternalOutput").ap()

    sch = Sched()
    stack = ExitStack()

    OFF_W = 0
    OFF_MX = 36864
    OFF_S1 = 65536
    OFF_S2 = 81920
    OFF_ST = 90112
    ARENA = 101376
    A = stack.enter_context(nc.sbuf_tensor("arena", [128, ARENA], BF16))

    def bview(off, n):
        return A[:, off:off + n]

    def fvw(off, n):
        return A[:, off:off + 2 * n].bitcast(F32)

    def stv(off_f32, n):
        return fvw(OFF_ST + 2 * off_f32, n)

    whg_v = [bview(OFF_W + k * E2, E2) for k in range(8)]
    wout_v = [bview(OFF_W + 24576 + k * D, D) for k in range(12)]
    wq1_v = [bview(OFF_W + k * H, H) for k in range(8)]
    wq2_v = [bview(OFF_W + 32768 + k * D, D) for k in range(32)]

    def mxf(t, p):
        return fvw(OFF_MX + t * 2048 + p * 1024, TM)
    CT = lambda p: mxf(0, p)
    ZT = lambda p: mxf(1, p)
    SGT = lambda p: mxf(2, p)
    GT = lambda p: mxf(3, p)
    VT = lambda p: mxf(4, p)
    HOT = lambda p: mxf(5, p)
    hbf_v = [bview(OFF_MX + 12288 + j * TM, TM) for j in range(12)]
    ln1_v2 = [[bview(OFF_MX + 18432 + ip * 4096 + k * TM, TM) for k in range(8)]
              for ip in range(2)]
    aWQ = lambda p: fvw(OFF_MX + 26624 + p * 1024, 512)
    aWQ2 = lambda p: stv(3072 + p * 512, 512)
    SX = lambda p, dc: fvw(OFF_S1 + p * 8192 + dc * 1024, TM)
    WB = lambda p: fvw(OFF_S1 + p * 8192, 4096)
    FX = lambda p, dc: fvw(OFF_S1 + p * 8192 + dc * 512, TF)
    UB = lambda hk: (bview(OFF_S1 + 4096 + hk * TF, TF) if hk < 16 else
                     bview(OFF_S1 + 12288 + (hk - 16) * TF, TF))
    X2T = lambda dm: fvw(OFF_S2 + dm * 1024, TM)
    QB = lambda p: bview(OFF_S2 + p * 1024, 1024)
    OT = lambda dm: fvw(OFF_S2 + dm * 512, TF)
    LB = lambda dc: bview(OFF_S2 + 4096 + dc * TF, TF)
    U2 = lambda p: bview(OFF_S2 + 6144 + p * TF, TF)
    SQ = lambda p: stv(p * 512, TM)
    SQF = lambda p: stv(p * 256, TF)
    maskWQ = stv(1024, 1024)
    sgnWQ = stv(2048, 1024)
    ti = stv(1024, 512)
    ti_i = bview(OFF_ST + 2 * 1024, 1024).bitcast(I32)
    ty = stv(1536, 512)
    ta = stv(2048, 512)
    vt = stv(2560, 512)
    YE = lambda q: stv(3072 + q * 512, TM)
    EPt = lambda q: stv(3072 + q * 256, TF)
    m1 = stv(4096, 512)
    r1 = stv(4608, 512)
    mr1 = stv(5120, 512)
    m2 = stv(4096, 256); hm = stv(4352, 256)
    r2 = stv(4608, 256); hr = stv(4864, 256)
    mr2 = stv(5120, 256); nhmr = stv(5376, 256)

    def small(name, shape, dt=F32):
        return stack.enter_context(nc.sbuf_tensor(name, shape, dt))
    sm_bhg = small("sm_bhg", [128, 24])
    sm_negbg = small("sm_negbg", [128, 12])
    sm_bh05 = small("sm_bh05", [128, 12])
    sm_bout = small("sm_bout", [128, 8])
    sm_h0 = small("sm_h0", [128, 12])
    sm_tmp = small("sm_tmp", [128, 12])
    carry = small("carry", [128, 12])
    sm_lnw = small("sm_lnw", [128, 32])
    sm_lnb = small("sm_lnb", [128, 32])
    sm_rw = small("sm_rw", [128, 32])
    sm_bw = small("sm_bw", [128, 32])
    sm_bwb = small("sm_bwb", [128, 32], BF16)
    sm_bwb2 = small("sm_bwb2", [128, 64], BF16)
    ones_f = small("ones_f", [128, 128])
    ones_8 = small("ones_8", [128, 8])
    ones_b = small("ones_b", [128, 128], BF16)
    magic = small("magic", [128, 1], I32)
    gpart = small("gpart", [128, 64])
    gpartB = small("gpartB", [128, 64])
    g2 = small("g2", [128, 2])
    gev = small("gev", [128, 2])
    gs = small("gs", [128, 2])
    sm_gam = small("sm_gam", [128, 2])
    sm_thr = small("sm_thr", [128, 2])
    sm_nthr = small("sm_nthr", [128, 2])
    sm_nlnw = small("sm_nlnw", [128, 32])
    s1sb = small("s1sb", [128, 8])
    s2sb = small("s2sb", [128, 8])

    pb = [stack.enter_context(nc.psum_tensor(f"pb{i}", [128, 512], F32))
          for i in range(8)]

    SEMS = ["pe", "act", "dve", "pool", "ld0", "ld1", "wld0", "wld1",
            "sm", "fw", "st", "qst"] + [f"sl{k}" for k in range(8)]

    o = sch.op
    dma = sch.dma

    for name, dst, src in [("bhg", sm_bhg, bhg2), ("bout", sm_bout, bout2),
                           ("h0", sm_h0, h02), ("lnw", sm_lnw, lnw2),
                           ("lnb", sm_lnb, lnb2), ("onesf", ones_f, onesf),
                           ("ones8", ones_8, ones8), ("onesb", ones_b, onesb)]:
        dma("sync", dst[:], src, mark=f"sm_{name}", sem="sm")
    SM_ALL = ("sm", 8 * 16)
    o("vector", lambda E: E.memset(magic[:], MAGIC))
    o("scalar", lambda E: E.activation(sm_tmp[:], sm_h0[:], AF.Sigmoid),
      waits=[SM_ALL], mark="pro_sig")
    o("vector", lambda E: E.tensor_scalar(carry[:], sm_h0[:], 0.5, None, ALU.add),
      waits=[SM_ALL])
    o("vector", lambda E: E.tensor_tensor(carry[:], carry[:], sm_tmp[:], ALU.max),
      waits=["pro_sig"], mark="pro_carry")
    o("vector", lambda E: E.tensor_scalar(sm_negbg[:], sm_bhg[:, 12:24], -1.0, None, ALU.mult))
    o("vector", lambda E: E.tensor_scalar(sm_bh05[:], sm_bhg[:, 0:12], 0.5, None, ALU.add))
    o("vector", lambda E: E.tensor_scalar(sm_nlnw[:], sm_lnw[:], -1.0, None, ALU.mult))
    o("vector", lambda E: E.reciprocal(sm_rw[:], sm_lnw[:]), mark="pro_rw")
    o("gpsimd", lambda E: E.tensor_tensor(sm_bw[:], sm_lnb[:], sm_rw[:], ALU.mult),
      waits=["pro_rw"])
    o("gpsimd", lambda E: E.tensor_copy(sm_bwb[:], sm_bw[:]))
    o("gpsimd", lambda E: E.memset(sm_bwb2[:], 1.0))
    o("gpsimd", lambda E: E.tensor_copy(
        sm_bwb2[:].rearrange("p (k t) -> p k t", t=2)[:, :, 1:2],
        sm_bw[:].rearrange("p (k o) -> p k o", o=1)), mark="pro_done")

    SL = lambda k: fvw(OFF_S1 + k * 2048, 1024)
    A_d = fvw(OFF_S2 + 0, 1024)
    A_p = fvw(OFF_S2 + 2048, 1024)
    MN_p = fvw(OFF_S2 + 4096, 1024)
    qb_d = bview(OFF_S2 + 6144, 1024)
    qb_p = bview(OFF_S2 + 7168, 1024)
    gslot = [0]

    def nslot():
        k = gslot[0] % 8
        gslot[0] += 1
        return k

    def wsub_src(s):
        if s < 32:
            dc, qq = s // 4, s % 4
            return wfcT[dc * 128:(dc + 1) * 128, qq * 1024:(qq + 1) * 1024]
        hk = s - 32
        return wprojT[hk * 128:(hk + 1) * 128, :]

    def mw_unit(m):
        k = nslot()
        if m < 24:
            dc, q = m // 3, m % 3
            dma("sync", SL(k)[:], whgT[dc * 128:(dc + 1) * 128, q * 1024:(q + 1) * 1024],
                sem=f"sl{k}", writes=[f"sl{k}"])
            o("scalar", (lambda kk, d, qq: (lambda E: E.copy(
                whg_v[d][:, qq * 1024:(qq + 1) * 1024], SL(kk)[:])))(k, dc, q),
              reads=[f"sl{k}"], mark=f"whg{dc}" if q == 2 else None)
        else:
            kj = m - 24
            dma("sync", SL(k)[:], woutT[kj * 128:(kj + 1) * 128, :],
                sem=f"sl{k}", writes=[f"sl{k}"])
            o("scalar", (lambda kk, d: (lambda E: E.copy(wout_v[d][:], SL(kk)[:])))(k, kj),
              reads=[f"sl{k}"], mark=f"wout{kj}")

    def p1_unit(s):
        k = nslot()
        dma("sync", SL(k)[:], wsub_src(s), sem=f"sl{k}", writes=[f"sl{k}"])
        o("vector", (lambda ss, kk: (lambda E: E.reduce_sum(
            gpart[:, ss:ss + 1], SL(kk)[:], AX, apply_absolute_value=True)))(s, k),
          reads=[f"sl{k}"], mark=f"wqr_{s}")

    def p2_unit(s):
        k = nslot()
        qp = s % 2
        qb = qb_d if qp else qb_p
        qslot = "qbd" if qp else "qbp"
        w = 0 if s < 32 else 1
        dma("sync", SL(k)[:], wsub_src(s), sem=f"sl{k}", writes=[f"sl{k}"])
        o("scalar", (lambda kk: (lambda E: E.activation(A_d[:], SL(kk)[:], AF.Abs)))(k),
          reads=[f"sl{k}"], writes=["aqd"])
        o("scalar", (lambda kk: (lambda E: E.sign(A_p[:], SL(kk)[:])))(k),
          reads=[f"sl{k}"], writes=["sgq"])
        o("vector", (lambda ww: (lambda E: E.tensor_single_scalar(
            MN_p[:], A_d[:], sm_thr[:, ww:ww + 1], ALU.is_gt)))(w),
          waits=[f"thr_{w}"], reads=["aqd"], writes=["mnp"])
        if s >= 32:
            hk = s - 32
            o("vector", (lambda kk2: (lambda E: E.tensor_scalar(
                MN_p[:], MN_p[:], sm_lnw[:, kk2:kk2 + 1], None, ALU.mult)))(hk),
              writes=["mnp"])
        o("vector", (lambda q: (lambda E: E.tensor_tensor(
            q[:], MN_p[:], A_p[:], ALU.mult)))(qb),
          reads=["mnp", "sgq"], writes=[qslot], mark=f"wqq_{s}")
        if s < 32:
            dc, qq4 = s // 4, s % 4
            dma("gpsimd", wq1d[dc * 128:(dc + 1) * 128, qq4 * 1024:(qq4 + 1) * 1024],
                qb[:], sem="qst", reads=[qslot])
        else:
            hk = s - 32
            dma("gpsimd", wq2d[hk * 128:(hk + 1) * 128, :], qb[:],
                sem="qst", reads=[qslot])
            for dm in range(8):
                o("tensor", (lambda qq, dd, kk: (lambda E: E.matmul(
                    pb[dd][:, 0:2], qq[:, dd * 128:(dd + 1) * 128],
                    sm_bwb2[:, 2 * kk:2 * kk + 2], start=(kk == 0), stop=(kk == 31),
                    skip_group_check=True)))(qb, dm, hk),
                  waits=(["pro_done"] if (hk == 0 and dm == 0) else []),
                  reads=[qslot],
                  writes=([f"bank{dm}"] if hk == 0 else []),
                  mark=f"s1mm_{hk}_{dm}")

    mwq = list(range(36))
    p1q = list(range(64))
    while p1q or mwq:
        for _ in range(2):
            if p1q:
                p1_unit(p1q.pop(0))
        if mwq:
            mw_unit(mwq.pop(0))
    for w in range(2):
        o("gpsimd", (lambda ww: (lambda E: E.tensor_copy(
            gpartB[:, 32 * ww:32 * ww + 32], gpart[:, 32 * ww:32 * ww + 32])))(w),
          waits=[f"wqr_{32 * w + 30}", f"wqr_{32 * w + 31}"], mark=f"g2c_{w}")
        o("vector", (lambda ww: (lambda E: E.reduce_sum(
            g2[:, ww:ww + 1], gpartB[:, 32 * ww:32 * ww + 32], AX)))(w),
          waits=[f"g2c_{w}"], mark=f"g2_{w}")
        o("tensor", (lambda ww: (lambda E: E.matmul(
            pb[0][0:8, ww:ww + 1], ones_8[:], g2[:, ww:ww + 1], start=True, stop=True)))(w),
          waits=[f"g2_{w}", SM_ALL], writes=["bank0"], mark=f"gmm1_{w}")
        o("scalar", (lambda ww: (lambda E: E.copy(gev[0:8, ww:ww + 1], pb[0][0:8, ww:ww + 1])))(w),
          waits=[f"gmm1_{w}"], reads=["bank0"], mark=f"gev_{w}")
        o("tensor", (lambda ww: (lambda E: E.matmul(
            pb[0][0:1, ww:ww + 1], gev[0:8, ww:ww + 1], ones_f[0:8, 0:1], start=True, stop=True)))(w),
          waits=[f"gev_{w}"], writes=["bank0"], mark=f"gmm2_{w}")
        o("scalar", (lambda ww: (lambda E: E.copy(gs[0:1, ww:ww + 1], pb[0][0:1, ww:ww + 1])))(w),
          waits=[f"gmm2_{w}"], reads=["bank0"], mark=f"gs_{w}")
        o("tensor", (lambda ww: (lambda E: E.matmul(
            pb[0][:, ww:ww + 1], ones_f[0:1, 0:128], gs[0:1, ww:ww + 1], start=True, stop=True)))(w),
          waits=[f"gs_{w}"], writes=["bank0"], mark=f"gmm3_{w}")
        o("scalar", (lambda ww: (lambda E: E.activation(
            sm_gam[:, ww:ww + 1], pb[0][:, ww:ww + 1], AF.Copy,
            bias=1e-6, scale=1.0 / NW)))(w),
          waits=[f"gmm3_{w}"], reads=["bank0"], mark=f"gam_{w}")
        o("vector", (lambda ww: (lambda E: E.tensor_scalar(
            sm_thr[:, ww:ww + 1], sm_gam[:, ww:ww + 1], 0.5, None, ALU.mult)))(w),
          waits=[f"gam_{w}"])
        o("vector", (lambda ww: (lambda E: E.tensor_scalar(
            sm_nthr[:, ww:ww + 1], sm_gam[:, ww:ww + 1], -0.5, None, ALU.mult)))(w),
          mark=f"thr_{w}")
    for s in range(64):
        p2_unit(s)
    for dm in range(8):
        o("scalar", (lambda dd: (lambda E: E.copy(
            s1sb[:, dd:dd + 1], pb[dd][:, 0:1])))(dm),
          waits=[f"s1mm_31_{dm}"], reads=[f"bank{dm}"],
          mark=f"s1e_{dm}")
        o("scalar", (lambda dd: (lambda E: E.copy(
            s2sb[:, dd:dd + 1], pb[dd][:, 1:2])))(dm),
          reads=[f"bank{dm}"], mark=f"s2e_{dm}")
    S12_DONE = "s2e_7"
    if debug_x2:
        dma("gpsimd", s12dbg[:, 0:8], s1sb[:], waits=[S12_DONE], sem="st")
        dma("gpsimd", s12dbg[:, 8:16], s2sb[:], waits=[S12_DONE], sem="st")
        dma("gpsimd", bwdbg, sm_bw[:], waits=["pro_done"], sem="st")
        dma("gpsimd", bw2dbg, sm_bwb2[:], waits=["pro_done"], sem="st")

    QST_ALL = ("qst", 16 * 64)
    n_x2_stores = [0]
    n_dbg_stores = [4 if debug_x2 else 0]

    def m_load(i):
        p = i % 2
        for dc in range(8):
            dma("sync", SX(p, dc)[:], xT[dc * 128:(dc + 1) * 128, i * TM:(i + 1) * TM],
                waits=([QST_ALL] if i < 2 else []),
                sem=f"ld{p}", writes=[f"sx{p}_{dc}"], mark=f"xld_{i}_{dc}")

    def m_stats(i):
        p = i % 2
        for dc in range(8):
            o("scalar", (lambda pp, d: (lambda E: E.activation(
                SQ(d % 2)[:], SX(pp, d)[:], AF.Square)))(p, dc),
              reads=[f"sx{p}_{dc}"], writes=[f"msq{dc % 2}"], mark=f"m_sq_{i}_{dc}")
            o("tensor", (lambda pp, d: (lambda E: E.matmul(
                pb[6][:], ones_f[:], SX(pp, d)[:], start=(d == 0), stop=(d == 7),
                skip_group_check=True)))(p, dc),
              reads=[f"sx{p}_{dc}"], writes=(["bank6"] if dc == 0 else []),
              mark=f"m_st1_{i}" if dc == 7 else None)
            o("tensor", (lambda d: (lambda E: E.matmul(
                pb[7][:], ones_f[:], SQ(d % 2)[:], start=(d == 0), stop=(d == 7),
                skip_group_check=True)))(dc),
              reads=[f"msq{dc % 2}"], writes=(["bank7"] if dc == 0 else []),
              mark=f"m_st2_{i}" if dc == 7 else None)
        o("scalar", lambda E: E.activation(m1[:], pb[6][:], AF.Copy, scale=1.0 / D),
          waits=[f"m_st1_{i}"], reads=["bank6"], writes=["m1s"], mark=f"m_mu_{i}")
        o("scalar", lambda E: E.activation(mr1[:], m1[:], AF.Square),
          reads=["m1s"], writes=["mr1s"], mark=f"m_msq_{i}")

    def m_stats_back(i):
        p = i % 2
        o("vector", lambda E: E.tensor_scalar(vt[:], pb[7][:], 1.0 / D, 1e-5, ALU.mult, ALU.add),
          waits=[f"m_st2_{i}"], reads=["bank7"], mark=f"m_vt_{i}")
        o("vector", lambda E: E.tensor_sub(vt[:], vt[:], mr1[:]),
          reads=["mr1s"])
        o("vector", lambda E: E.tensor_single_scalar(ti_i[:, 0:512], vt[:].bitcast(I32), 1, ALU.arith_shift_right))
        o("vector", lambda E: E.scalar_tensor_tensor(
            ti_i[:, 0:512], ti_i[:, 0:512], -1, magic[:, 0:1].broadcast_to([128, 512]),
            ALU.mult, ALU.add))
        yv = ti_i[:, 0:512].bitcast(F32)
        for it in range(2):
            dst = r1[:] if it == 1 else yv
            o("vector", lambda E: E.tensor_mul(ta[:], vt[:], yv))
            o("vector", lambda E: E.tensor_mul(ta[:], ta[:], yv))
            o("vector", lambda E: E.tensor_scalar(ta[:], ta[:], -0.5, 1.5, ALU.mult, ALU.add))
            o("vector", (lambda dd: (lambda E: E.tensor_mul(dd, yv, ta[:])))(dst),
              writes=(["r1s"] if it == 1 else []),
              mark=f"m_r1_{i}" if it == 1 else None)
        o("vector", lambda E: E.tensor_mul(mr1[:], m1[:], r1[:]),
          reads=["m1s", "r1s"], writes=["mr1s"], mark=f"m_mr1_{i}")
        for dc in range(8):
            tmp = ti if dc % 2 == 0 else ty
            o("vector", (lambda pp, d, t: (lambda E: E.tensor_mul(t[:], SX(pp, d)[:], r1[:])))(p, dc, tmp),
              reads=[f"sx{p}_{dc}", "r1s"])
            o("vector", (lambda pp, d, t: (lambda E: E.tensor_sub(ln1_v2[pp][d][:], t[:], mr1[:])))(p, dc, tmp),
              reads=["mr1s"], writes=[f"ln1_{p}_{dc}"], mark=f"m_ln1_{i}_{dc}")
        if debug_x2:
            for dc in range(8):
                dma("gpsimd", lndbg[dc * 128:(dc + 1) * 128, i * TM:(i + 1) * TM],
                    ln1_v2[p][dc][:], sem="st", reads=[f"ln1_{p}_{dc}"])
                n_dbg_stores[0] += 1

    def m_mixer(i, hook5=None, hook7=None):
        for j in range(12):
            if j == 6 and hook5 is not None:
                hook5()
            if j == 8 and hook7 is not None:
                hook7()
            jp = j % 2
            ip = i % 2
            for dc in range(8):
                o("tensor", (lambda jj, d, b, q: (lambda E: E.matmul(
                    pb[b][:], whg_v[d][:, jj * 128:(jj + 1) * 128], ln1_v2[q][d][:],
                    start=(d == 0), stop=(d == 7), skip_group_check=True)))(j, dc, jp, ip),
                  waits=[f"whg{dc}"] if (i == 0 and j == 0) else [],
                  reads=[f"ln1_{ip}_{dc}"], writes=([f"bank{jp}"] if dc == 0 else []),
                  mark=f"m_Ah_{i}_{j}" if dc == 7 else None)
            for dc in range(8):
                o("tensor", (lambda jj, d, b, q: (lambda E: E.matmul(
                    pb[2 + b][:], whg_v[d][:, (12 + jj) * 128:(13 + jj) * 128], ln1_v2[q][d][:],
                    start=(d == 0), stop=(d == 7), skip_group_check=True)))(j, dc, jp, ip),
                  reads=[f"ln1_{ip}_{dc}"], writes=([f"bank{2 + jp}"] if dc == 0 else []),
                  mark=f"m_Ag_{i}_{j}" if dc == 7 else None)
            o("scalar", (lambda jj, b: (lambda E: E.activation(
                SGT(b)[:], pb[b][:], AF.Sigmoid, bias=sm_bhg[:, jj:jj + 1])))(j, jp),
              waits=[f"m_Ah_{i}_{j}", "pro_done"],
              writes=[f"sgt{jp}"], mark=f"m_sg_{i}_{j}")
            o("vector", (lambda jj, b: (lambda E: E.scalar_tensor_tensor(
                GT(b)[:], pb[b][:], sm_bh05[:, jj:jj + 1], SGT(b)[:],
                ALU.add, ALU.max)))(j, jp),
              waits=[f"m_Ah_{i}_{j}"],
              reads=[f"sgt{jp}", f"bank{jp}"], writes=[f"gt{jp}"], mark=f"m_g_{i}_{j}")
            o("scalar", (lambda jj, b: (lambda E: E.activation(
                CT(b)[:], pb[2 + b][:], AF.Sigmoid, scale=-1.0,
                bias=sm_negbg[:, jj:jj + 1])))(j, jp),
              waits=[f"m_Ag_{i}_{j}"],
              writes=[f"ct{jp}"], mark=f"m_c_{i}_{j}")
            o("scalar", (lambda jj, b: (lambda E: E.activation(
                ZT(b)[:], pb[2 + b][:], AF.Sigmoid, bias=sm_bhg[:, 12 + jj:13 + jj])))(j, jp),
              reads=[f"bank{2 + jp}"], writes=[f"zt{jp}"], mark=f"m_z_{i}_{j}")
            o("gpsimd", (lambda b: (lambda E: E.tensor_mul(VT(b)[:], ZT(b)[:], GT(b)[:])))(jp),
              reads=[f"zt{jp}", f"gt{jp}"], writes=[f"vtm{jp}"], mark=f"m_v_{i}_{j}")
            o("vector", (lambda jj, b: (lambda E: E.tensor_tensor_scan(
                HOT(b)[:], CT(b)[:], VT(b)[:], carry[:, jj:jj + 1],
                ALU.mult, ALU.add)))(j, jp),
              waits=["pro_carry"] if i == 0 else [],
              reads=[f"ct{jp}", f"vtm{jp}", f"carry{j}"], writes=[f"hot{jp}"],
              mark=f"m_scan_{i}_{j}")
            o("gpsimd", (lambda jj, b: (lambda E: E.tensor_copy(
                carry[:, jj:jj + 1], HOT(b)[:, TM - 1:TM])))(j, jp),
              waits=[f"m_scan_{i}_{j}"],
              reads=[f"hot{jp}"], writes=[f"carry{j}"], mark=f"m_carry_{i}_{j}")
            o("gpsimd", (lambda jj, b: (lambda E: E.tensor_copy(hbf_v[jj][:], HOT(b)[:])))(j, jp),
              reads=[f"hot{jp}"], writes=[f"hbf{j}"], mark=f"m_hbf_{i}_{j}")
            if debug_x2:
                dma("gpsimd", hdbg[j * 128:(j + 1) * 128, i * TM:(i + 1) * TM],
                    hbf_v[j][:], sem="st", reads=[f"hbf{j}"])
                n_dbg_stores[0] += 1
        if debug_x2:
            dma("gpsimd", cdbg[:, i * 12:(i + 1) * 12], carry[:],
                waits=[f"m_scan_{i}_{11}"], sem="st")
            n_dbg_stores[0] += 1

    def m_out(i):
        p = i % 2
        for dm in range(8):
            bp = dm % 2
            for kj in range(12):
                o("tensor", (lambda d, k, b: (lambda E: E.matmul(
                    pb[4 + b][:], wout_v[k][:, d * 128:(d + 1) * 128], hbf_v[k][:],
                    start=(k == 0), stop=(k == 11), skip_group_check=True)))(dm, kj, bp),
                  waits=[f"wout{kj}"] if (i == 0 and dm == 0) else [],
                  reads=[f"hbf{kj}"], writes=([f"bank{4 + bp}"] if kj == 0 else []),
                  mark=f"m_B_{i}_{dm}" if kj == 11 else None)
            wl = [f"m_B_{i}_{dm}"]
            if dm == 0 and i > 0:
                wl.append(("st", 16 * (n_x2_stores[0] + n_dbg_stores[0])))
            if dm == 0 and i == 0:
                wl += [S12_DONE, QST_ALL]
            o("vector", (lambda pp, d, b: (lambda E: E.scalar_tensor_tensor(
                X2T(d)[:], pb[4 + b][:], sm_bout[:, d:d + 1], SX(pp, d)[:],
                ALU.add, ALU.add)))(p, dm, bp),
              waits=wl, reads=[f"bank{4 + bp}", f"sx{p}_{dm}"],
              mark=f"m_x2_{i}_{dm}")
            dma("gpsimd", x2s[dm * 128:(dm + 1) * 128, i * TM:(i + 1) * TM],
                X2T(dm)[:], waits=[f"m_x2_{i}_{dm}"], sem="st")
            n_x2_stores[0] += 1

    m_load(0)
    m_stats(0)
    m_stats_back(0)
    for i in range(NTM):
        if i + 1 < NTM:
            m_load(i + 1)
        m_mixer(i)
        if i + 1 < NTM:
            m_stats(i + 1)
        m_out(i)
        if i + 1 < NTM:
            m_stats_back(i + 1)
    X2_ALL = ("st", 16 * (n_x2_stores[0] + n_dbg_stores[0]))
    M_DONE = f"m_x2_{NTM - 1}_7"

    LAST = NTM - 1
    for dc in range(8):
        wl = [f"m_Ah_{LAST}_11", f"m_Ag_{LAST}_11", QST_ALL]
        if dc >= 6:
            wl.append(f"m_B_{LAST}_7")
        dma("sync", wq1_v[dc][:], wq1d[dc * 128:(dc + 1) * 128, :],
            waits=wl, sem="fw")
    for hk in range(32):
        dma("sync", wq2_v[hk][:], wq2d[hk * 128:(hk + 1) * 128, :],
            waits=[f"m_B_{LAST}_7", f"m_hbf_{LAST}_11", QST_ALL], sem="fw")
    FW1 = ("fw", 16 * 8)
    FW2 = ("fw", 16 * 40)

    n_out_stores = [0]

    def bankof(dm):
        return 4 + dm if dm < 4 else dm - 4

    def f_load(f):
        p = f % 2
        if f == 0:
            fwl = [("st", 16 * 8), f"m_x2_{NTM - 2}_7"]
        elif f == 1:
            fwl = [("st", 16 * 8), f"m_x2_{NTM - 1}_7"]
        else:
            fwl = []
        for dc in range(8):
            dma("sync", FX(p, dc)[:], x2s[dc * 128:(dc + 1) * 128, f * TF:(f + 1) * TF],
                waits=fwl,
                sem=f"ld{p}", writes=[f"fx{p}_{dc}"], mark=f"fxld_{f}_{dc}")

    def newton_f(src, dst, dstslot, mk):
        o("vector", (lambda s: (lambda E: E.tensor_single_scalar(
            ti_i[:, 0:TF], s.bitcast(I32), 1, ALU.arith_shift_right)))(src))
        o("vector", lambda E: E.scalar_tensor_tensor(
            ti_i[:, 0:TF], ti_i[:, 0:TF], -1, magic[:, 0:1].broadcast_to([128, TF]),
            ALU.mult, ALU.add))
        yv2 = ti_i[:, 0:TF].bitcast(F32)
        for it in range(2):
            dd = dst if it == 1 else yv2
            o("vector", (lambda s: (lambda E: E.tensor_mul(ta[:, 0:TF], s, yv2)))(src))
            o("vector", lambda E: E.tensor_mul(ta[:, 0:TF], ta[:, 0:TF], yv2))
            o("vector", lambda E: E.tensor_scalar(ta[:, 0:TF], ta[:, 0:TF], -0.5, 1.5, ALU.mult, ALU.add))
            o("vector", (lambda d2: (lambda E: E.tensor_mul(d2, yv2, ta[:, 0:TF])))(dd),
              writes=([dstslot] if it == 1 else []),
              mark=mk if it == 1 else None)

    def f_stats(f):
        p = f % 2
        for dc in range(8):
            o("scalar", (lambda pp, d: (lambda E: E.activation(
                SQF(d % 2)[:], FX(pp, d)[:], AF.Square)))(p, dc),
              reads=[f"fx{p}_{dc}"], writes=[f"fsq{dc % 2}"], mark=f"f_sq_{f}_{dc}")
            o("tensor", (lambda pp, d: (lambda E: E.matmul(
                pb[6][:, 0:TF], ones_f[:], FX(pp, d)[:], start=(d == 0), stop=(d == 7),
                skip_group_check=True)))(p, dc),
              reads=[f"fx{p}_{dc}"], writes=(["bank6"] if dc == 0 else []),
              mark=f"f_st1_{f}" if dc == 7 else None)
            o("tensor", (lambda d: (lambda E: E.matmul(
                pb[7][:, 0:TF], ones_f[:], SQF(d % 2)[:], start=(d == 0), stop=(d == 7),
                skip_group_check=True)))(dc),
              reads=[f"fsq{dc % 2}"], writes=(["bank7"] if dc == 0 else []),
              mark=f"f_st2_{f}" if dc == 7 else None)
        o("scalar", lambda E: E.activation(m2[:], pb[6][:, 0:TF], AF.Copy, scale=1.0 / D),
          waits=[f"f_st1_{f}"], reads=["bank6"], writes=["m2s"], mark=f"f_mu_{f}")
        o("scalar", lambda E: E.activation(mr2[:], m2[:], AF.Square),
          reads=["m2s"], writes=["mr2s"], mark=f"f_msq_{f}")
        o("vector", lambda E: E.tensor_scalar(vt[:, 0:TF], pb[7][:, 0:TF], 1.0 / D, 1e-5, ALU.mult, ALU.add),
          waits=[f"f_st2_{f}"], reads=["bank7"], mark=f"f_vt_{f}")
        o("vector", lambda E: E.tensor_sub(vt[:, 0:TF], vt[:, 0:TF], mr2[:]),
          reads=["mr2s"])
        newton_f(vt[:, 0:TF], r2[:], "r2s", f"f_r2_{f}")
        o("vector", lambda E: E.tensor_mul(mr2[:], m2[:], r2[:]),
          reads=["m2s", "r2s"], writes=["mr2s"], mark=f"f_mr2_{f}")
        for dc in range(8):
            tmp = ti if dc % 2 == 0 else ty
            o("vector", (lambda pp, d, t: (lambda E: E.tensor_mul(t[:, 0:TF], FX(pp, d)[:], r2[:])))(p, dc, tmp),
              reads=[f"fx{p}_{dc}", "r2s"])
            o("vector", (lambda d, t: (lambda E: E.tensor_sub(LB(d)[:], t[:, 0:TF], mr2[:])))(dc, tmp),
              reads=["mr2s"], writes=[f"lb{dc}"], mark=f"f_ln2_{f}_{dc}")
        if debug_x2:
            for dc in range(8):
                dma("gpsimd", ldbg[dc * 128:(dc + 1) * 128, f * TF:(f + 1) * TF],
                    LB(dc)[:], sem="st", reads=[f"lb{dc}"])
                n_dbg_stores[0] += 1

    def f_mid(f):
        def su_pair(k):
            kp = k % 2
            o("tensor", (lambda kk: (lambda E: E.matmul(
                pb[2][:, 0:TF], ones_b[:], UB(kk)[:], start=(kk == 0), stop=(kk == 31),
                skip_group_check=True)))(k),
              reads=[f"ub{k}"], writes=(["bank2"] if k == 0 else []),
              mark=f"f_su_{f}" if k == 31 else None)
            o("tensor", (lambda kk, b: (lambda E: E.matmul(
                pb[3][:, 0:TF], ones_b[:], U2(b)[:], start=(kk == 0), stop=(kk == 31),
                skip_group_check=True)))(k, kp),
              reads=[f"u2{kp}"], writes=(["bank3"] if k == 0 else []),
              mark=f"f_su2_{f}" if k == 31 else None)

        for hk in range(32):
            hp = hk % 2
            for dc in range(8):
                o("tensor", (lambda k, d, b: (lambda E: E.matmul(
                    pb[b][:, 0:TF], wq1_v[d][:, k * 128:(k + 1) * 128], LB(d)[:],
                    start=(d == 0), stop=(d == 7), skip_group_check=True)))(hk, dc, hp),
                  waits=[FW1] if (f == 0 and hk == 0 and dc == 0) else [],
                  reads=[f"lb{dc}"], writes=([f"bank{hp}"] if dc == 0 else []),
                  mark=f"f_F1_{f}_{hk}" if dc == 7 else None)
            o("scalar", (lambda k, b: (lambda E: E.activation(UB(k)[:], pb[b][:, 0:TF], AF.Gelu)))(hk, hp),
              waits=[f"f_F1_{f}_{hk}"],
              reads=[f"bank{hp}"], writes=[f"ub{hk}"], mark=f"f_gel_{f}_{hk}")
            o("scalar", (lambda k, b: (lambda E: E.activation(U2(b)[:], UB(k)[:], AF.Square)))(hk, hp),
              reads=[f"ub{hk}"], writes=[f"u2{hp}"], mark=f"f_usq_{f}_{hk}")
            if hk >= 1:
                su_pair(hk - 1)
        su_pair(31)
        if debug_x2:
            for hk2 in range(32):
                dma("gpsimd", udbg[hk2 * 128:(hk2 + 1) * 128, f * TF:(f + 1) * TF],
                    UB(hk2)[:], sem="st", reads=[f"ub{hk2}"])
                n_dbg_stores[0] += 1
        o("scalar", lambda E: E.activation(hm[:], pb[2][:, 0:TF], AF.Copy, scale=1.0 / H),
          waits=[f"f_su_{f}"], reads=["bank2"], writes=["hms"], mark=f"f_hmu_{f}")
        o("scalar", lambda E: E.activation(nhmr[:], hm[:], AF.Square),
          reads=["hms"], writes=["nhmrs"], mark=f"f_hmsq_{f}")
        o("vector", lambda E: E.tensor_scalar(vt[:, 0:TF], pb[3][:, 0:TF], 1.0 / H, 1e-5, ALU.mult, ALU.add),
          waits=[f"f_su2_{f}"], reads=["bank3"], mark=f"f_hvt_{f}")
        o("vector", lambda E: E.tensor_sub(vt[:, 0:TF], vt[:, 0:TF], nhmr[:]),
          reads=["nhmrs"])
        newton_f(vt[:, 0:TF], hr[:], "hrs", f"f_hr_{f}")
        o("vector", lambda E: E.tensor_mul(nhmr[:], hm[:], hr[:]),
          reads=["hms", "hrs"], writes=["nhmrs"])
        o("vector", lambda E: E.tensor_scalar(nhmr[:], nhmr[:], -1.0, None, ALU.mult),
          writes=["nhmrs"], mark=f"f_negmr_{f}")

    def f_f2(f, dms):
        for dm in dms:
            bk = bankof(dm)
            for hk in range(32):
                o("tensor", (lambda d, k, b: (lambda E: E.matmul(
                    pb[b][:, 0:TF], wq2_v[k][:, d * 128:(d + 1) * 128], UB(k)[:],
                    start=(k == 0), stop=(k == 31), skip_group_check=True)))(dm, hk, bk),
                  waits=[FW2] if (f == 0 and dm == 0 and hk == 0) else [],
                  reads=[f"ub{hk}"], writes=([f"bank{bk}"] if hk == 0 else []),
                  mark=f"f_F2_{f}_{dm}" if hk == 31 else None)

    def f_epi(f, dms):
        p = f % 2
        for dm in dms:
            bk = bankof(dm)
            ep = EPt(dm % 2)
            wl = [f"f_F2_{f}_{dm}", f"f_hr_{f}"]
            if dm == 0 and f > 0:
                wl.append(("st", 16 * (n_x2_stores[0] + n_dbg_stores[0] + n_out_stores[0])))
            o("vector", (lambda b, e: (lambda E: E.tensor_mul(e[:], pb[b][:, 0:TF], hr[:])))(bk, ep),
              waits=wl, reads=["hrs", f"bank{bk}"], writes=[f"ep{dm % 2}"])
            o("vector", (lambda d, e: (lambda E: E.scalar_tensor_tensor(
                e[:], nhmr[:], s1sb[:, d:d + 1], e[:], ALU.mult, ALU.add)))(dm, ep),
              waits=[S12_DONE, f"f_negmr_{f}"], reads=["nhmrs"])
            o("vector", (lambda pp, d, e: (lambda E: E.scalar_tensor_tensor(
                OT(d)[:], e[:], s2sb[:, d:d + 1], FX(pp, d)[:], ALU.add, ALU.add)))(p, dm, ep),
              reads=[f"fx{p}_{dm}", f"ep{dm % 2}"],
              mark=f"f_ep_{f}_{dm}")
            dma("gpsimd", yT[dm * 128:(dm + 1) * 128, f * TF:(f + 1) * TF],
                OT(dm)[:], waits=[f"f_ep_{f}_{dm}"], sem="st")
            n_out_stores[0] += 1

    f_load(0)
    f_stats(0)
    for f in range(NTF):
        if f + 1 < NTF:
            f_load(f + 1)
        f_mid(f)
        f_f2(f, [0, 1, 2, 3])
        f_epi(f, [0, 1, 2, 3])
        if f + 1 < NTF:
            f_stats(f + 1)
        f_f2(f, [4, 5, 6, 7])
        f_epi(f, [4, 5, 6, 7])
    sch.fence("sync", "st", 16 * (n_x2_stores[0] + n_dbg_stores[0] + n_out_stores[0]))

    semh = {}
    for sname in SEMS:
        semh[sname] = stack.enter_context(nc.semaphore(sname))
    block = stack.enter_context(nc.Block())

    def emitter(name):
        def run(E):
            for ws, emit, marks in sch.q[name]:
                for sem, val in ws:
                    E.wait_ge(semh[sem], val)
                if emit is not None:
                    ins = emit(E)
                    for sem, amt in marks:
                        ins.then_inc(semh[sem], amt)
        return run

    block.sync(emitter("sync"))
    block.tensor(emitter("tensor"))
    block.scalar(emitter("scalar"))
    block.vector(emitter("vector"))
    block.gpsimd(emitter("gpsimd"))
    stack.close()
    return nc


def kernel(**inputs):
    x = np.asarray(inputs["L0_attn_x"], np.float32)
    whgT = np.ascontiguousarray(np.asarray(inputs["L0_attn_W_hg"], np.float32).T)
    woutT = np.ascontiguousarray(np.asarray(inputs["L0_attn_W_out"], np.float32).T)
    wfcT = np.ascontiguousarray(np.asarray(inputs["L0_ffn_W_fc"], np.float32).T)
    wprojT = np.ascontiguousarray(np.asarray(inputs["L0_ffn_W_proj"], np.float32).T)
    bhg2 = np.ascontiguousarray(np.asarray(inputs["L0_attn_b_hg"], np.float32).reshape(24, 128).T)
    bout2 = np.ascontiguousarray(np.asarray(inputs["L0_attn_b_out"], np.float32).reshape(8, 128).T)
    h02 = np.ascontiguousarray(np.asarray(inputs["L0_attn_h0"], np.float32).reshape(12, 128).T)
    lnw2 = np.ascontiguousarray(np.asarray(inputs["L0_ffn_ln_w"], np.float32).reshape(32, 128).T)
    lnb2 = np.ascontiguousarray(np.asarray(inputs["L0_ffn_ln_b"], np.float32).reshape(32, 128).T)
    onesf = np.ones((128, 128), np.float32)
    ones8v = np.zeros((128, 8), np.float32)
    for m in range(8):
        ones8v[m * 16:(m + 1) * 16, m] = 1.0
    onesb = np.ones((128, 128), ml_dtypes.bfloat16)

    nc = build_program()
    shared = dict(whgT=whgT, woutT=woutT, wfcT=wfcT, wprojT=wprojT,
                  bhg2=bhg2, bout2=bout2, h02=h02, lnw2=lnw2, lnb2=lnb2,
                  onesf=onesf, ones8=ones8v, onesb=onesb)
    in_maps = []
    for c in range(8):
        m = dict(shared)
        m["xT"] = np.ascontiguousarray(x[c].T)
        in_maps.append(m)
    res = run_bass_kernel_spmd(nc, in_maps, core_ids=list(range(8)))
    out = np.stack([np.asarray(res.results[c]["yT"], np.float32).T
                    for c in range(8)], axis=0)
    return np.ascontiguousarray(out)
